# version 01
# baseline (speedup 1.0000x reference)
import numpy as np
import concourse.bass as bass
import concourse.mybir as mybir
from contextlib import ExitStack

F32 = mybir.dt.float32
BF16 = mybir.dt.bfloat16
AF = mybir.ActivationFunctionType
ALU = mybir.AluOpType
AX = mybir.AxisListType


def default_cfg():
    return dict(S=4096, D=1024, H=16, HD=64, IP=22, NM=9, E=8,
                QSPLIT=(6, 6, 5, 5), EPS=1e-5)


def mini_cfg():
    return dict(S=1024, D=1024, H=16, HD=64, IP=4, NM=3, E=8,
                QSPLIT=(2, 2), EPS=1e-5)


def pack_weights(ws_gu, ws_d, cfg):
    D, IP = cfg["D"], cfg["IP"]
    DT = D // 128
    gus, ds = [], []
    for wgu, wd in zip(ws_gu, ws_d):
        a = np.ascontiguousarray(wgu).reshape(DT, 128, 2, IP, 128)
        a = a.transpose(1, 3, 2, 0, 4).reshape(128, IP * 2 * DT * 128)
        gus.append(a)
        d = np.ascontiguousarray(wd).reshape(IP, 128, D)
        d = d.transpose(1, 0, 2).reshape(128, IP * D)
        ds.append(d)
    return (np.ascontiguousarray(np.stack(gus)),
            np.ascontiguousarray(np.stack(ds)))


class MK:
    def __init__(self, cfg, debug_outputs=()):
        self.cfg = cfg
        self.S, self.D, self.H, self.HD = cfg["S"], cfg["D"], cfg["H"], cfg["HD"]
        self.IP, self.NM, self.E = cfg["IP"], cfg["NM"], cfg["E"]
        self.QSPLIT, self.EPS = cfg["QSPLIT"], cfg["EPS"]
        self.DT = self.D // 128
        self.TS = self.S // 128
        self.NB = self.S // 512
        self.QMAX = max(self.QSPLIT)
        self.NQ = len(self.QSPLIT)
        self.NPH = self.NM * self.NQ
        self.qbounds = []
        p = 0
        for nq in self.QSPLIT:
            self.qbounds.append((p, p + nq))
            p += nq
        assert p == self.IP
        self.debug_outputs = set(debug_outputs)

        self.nc = bass.Bass("TRN2", target_bir_lowering=False)
        self.es = ExitStack()
        self.cnt = {}
        self.sems = {}
        for name in ("dG0", "dG1", "dS", "dA", "dX", "dZ", "dW0", "dW1",
                     "dH20", "dH21", "dY0", "dY1", "dXT", "dLo0", "dLo1",
                     "dHT0", "dHT1", "dAO0", "dAO1",
                     "dAC0", "dAC1", "dAC2", "dAC3",
                     "dST0", "dST1", "dST2", "dST3",
                     "tT", "tV", "tS", "tG"):
            self.sems[name] = self.es.enter_context(self.nc.semaphore(name))
            self.cnt[name] = 0

    def inc(self, instr, name, n=1):
        instr.then_inc(self.sems[name], n)
        self.cnt[name] += n
        return self.cnt[name]

    def wait(self, eng, name, val):
        if val > 0:
            eng.wait_ge(self.sems[name], val)

    def sb(self, name, shape, dtype, stack=None):
        return (stack or self.es).enter_context(
            self.nc.sbuf_tensor(name, shape, dtype))

    def ps(self, name, shape, dtype=F32, stack=None):
        return (stack or self.es).enter_context(
            self.nc.psum_tensor(name, shape, dtype))

    def build(self):
        nc = self.nc
        S, D = self.S, self.D
        IP, NM, E = self.IP, self.NM, self.E

        self.x_h = nc.dram_tensor("x", [S, D], F32, kind="ExternalInput").ap()
        self.gw_h = nc.dram_tensor("gw", [D, E], F32, kind="ExternalInput").ap()
        self.wgu_h = nc.dram_tensor("wgu", [NM, 128, IP * 2 * self.DT * 128],
                                    F32, kind="ExternalInput").ap()
        self.wd_h = nc.dram_tensor("wd", [NM, 128, IP * D], F32,
                                   kind="ExternalInput").ap()
        self.y_h = nc.dram_tensor("y", [S, D], F32, kind="ExternalOutput").ap()
        self.h1_h = nc.dram_tensor("h1", [S, self.H * 128], BF16, kind="Internal").ap()
        self.h2_h = nc.dram_tensor("h2", [S, D], BF16, kind="Internal").ap()
        self.h2lo_h = nc.dram_tensor("h2lo", [S, D], BF16, kind="Internal").ap()
        self.ao_h = nc.dram_tensor("attn_out", [S, D], F32, kind="Internal").ap()
        self.nrm_h = nc.dram_tensor("nrm", [self.TS, self.H, 128], BF16,
                                    kind="Internal").ap()
        self.dbg = {}
        for name, (shape, dt) in {
            "aod": ([S, D], F32), "h2d": ([S, D], BF16),
            "cwd": ([S, E], F32),
        }.items():
            if name in self.debug_outputs:
                self.dbg[name] = nc.dram_tensor(name, shape, dt,
                                                kind="ExternalOutput").ap()

        H, TS, DT = self.H, self.TS, self.DT
        self.ident32 = self.sb("ident32", [128, 128], F32)
        self.identb = self.sb("identb", [128, 128], BF16)
        self.maskb = self.sb("maskb", [128, 4 * 512], BF16)
        self.mask32 = self.sb("mask32", [128, 512], F32)
        self.mx = self.sb("mx", [1, 1], F32)
        self.cw = self.sb("cw", [128, TS * E], F32)
        self.gwf = self.sb("gwf", [128, DT * E], F32)
        self.gwh = self.sb("gwh", [128, DT * E], BF16)
        self.gwl = self.sb("gwl", [128, DT * E], BF16)
        self.norms = self.sb("norms", [128, TS * H], BF16)
        self.xt = [self.sb(f"xt{i}", [128, D], F32) for i in (0, 1)]
        self.at = [self.sb(f"at{i}", [128, D], F32) for i in (0, 1)]
        self.h2t = [self.sb(f"h2t{i}", [128, D], BF16) for i in (0, 1)]
        self.lot = [self.sb(f"lot{i}", [128, D], BF16) for i in (0, 1)]
        self.msum = self.sb("msum", [128, 1], F32)
        self.negmean = self.sb("negmean", [128, 1], F32)
        self.sumsq = self.sb("sumsq", [128, 1], F32)
        self.varv = self.sb("varv", [128, 1], F32)
        self.lnt = self.sb("lnt", [128, 1], F32)
        self.rstd = self.sb("rstd", [128, 1], F32)
        self.rstd2 = self.sb("rstd2", [128, 1], F32)
        self.n2h = self.sb("n2h", [128, H], F32)
        self.nrmt = self.sb("nrmt", [128, H], BF16)
        self.mxp = self.sb("mxp", [128, 1], BF16)
        self.rcp4 = self.sb("rcp4", [128, 4], F32)

        self.esA = ExitStack()
        self.h_aug = self.sb("h_aug", [128, TS * H * 65], BF16, self.esA)
        self.hT_on = [self.sb(f"hTon{i}", [128, S], BF16, self.esA) for i in (0, 1)]
        self.hT_bd = [self.sb(f"hTbd{i}", [128, S], BF16, self.esA) for i in (0, 1)]
        self.bndr = [self.sb(f"bndr{i}", [1, S], BF16, self.esA) for i in (0, 1)]
        self.pT = [self.sb(f"pT{i}", [128, 1024], BF16, self.esA) for i in range(3)]
        self.oT = [self.sb(f"oT{i}", [65, 512], BF16, self.esA) for i in (0, 1)]
        self.asb = [self.sb(f"asb{i}", [128, 4 * self.HD], F32, self.esA)
                    for i in (0, 1)]
        self.nT = self.sb("nT", [H, 128], BF16, self.esA)
        self.zsb = self.sb("zsb", [128, H * 128], BF16, self.esA)

        self.esAP = ExitStack()
        self.qk_ps = [self.ps(f"qk{i}", [128, 1024], F32, self.esAP) for i in (0, 1)]
        self.o_ps = [self.ps(f"o{i}", [65, 512], F32, self.esAP) for i in (0, 1)]
        self.tr_ps = self.ps("tr", [128, 512], BF16, self.esAP)
        self.tr2_ps = self.ps("tr2", [128, 512], BF16, self.esAP)
        self.sqps = self.qk_ps[0]

        self.p0()
        self.p1()
        self.p2()
        self.p3()
        self.p4()
        for i in range(3):
            self.wait(self.nc.sync, f"dST{i}", self.cnt[f"dST{i}"])
        self.wait(self.nc.sync, "dS", self.cnt["dS"])
        self.es.close()
        return self.nc

    def p0(self):
        nc, g, v = self.nc, self.nc.gpsimd, self.nc.vector
        g.memset(self.ident32[:], 0.0)
        self.inc(g.engine_nop(), "tG")
        self.wait(g, "tG", self.cnt["tG"])
        ins = g.affine_select(out=self.ident32[:], in_=self.ident32[:],
                              compare_op=ALU.not_equal, fill=1.0, base=0,
                              pattern=[[-1, 128]], channel_multiplier=1)
        self.inc(ins, "tG")
        self.wait(v, "tG", self.cnt["tG"])
        self.inc(v.tensor_copy(self.identb[:], self.ident32[:]), "tV")
        for i in range(4):
            self.wait(g, "tV", self.cnt["tV"])
            g.memset(self.mask32[:], 1.0)
            self.inc(g.engine_nop(), "tG")
            self.wait(g, "tG", self.cnt["tG"])
            ins = g.affine_select(out=self.mask32[:], in_=self.mask32[:],
                                  compare_op=ALU.is_ge, fill=0.0,
                                  base=-128 * i, pattern=[[1, 512]],
                                  channel_multiplier=-1)
            self.inc(ins, "tG")
            self.wait(v, "tG", self.cnt["tG"])
            ins = v.tensor_copy(self.maskb[:, 512 * i:512 * (i + 1)],
                                self.mask32[:])
            self.inc(ins, "tV")
        ha = self.h_aug.ap().rearrange("p (t h c) -> p t h c", h=self.H, c=65)
        for h in range(self.H):
            g.memset(ha[:, :, h, 64:65], 1.0)
        g.memset(self.zsb[:], 0.0)
        self.g_ones = self.inc(g.engine_nop(), "tG")
        s_eng = nc.scalar
        self.wait(s_eng, "tG", self.g_ones)
        for t in range(self.TS):
            ins = s_eng.dma_start(
                self.h1_h[128 * t:128 * (t + 1), :], self.zsb[:])
            self.inc(ins, "dZ", 16)
        self.dZ_all = self.cnt["dZ"]
        ins = g.dma_start(
            self.gwf.ap().rearrange("p (dt e) -> p dt e", e=self.E),
            self.gw_h.rearrange("(dt p) e -> p dt e", p=128))
        self.inc(ins, "dX", 16)
        self.wait(v, "dX", self.cnt["dX"])
        self.inc(v.tensor_copy(self.gwh[:], self.gwf[:]), "tV")
        self.wait(v, "tV", self.cnt["tV"])
        self.inc(v.tensor_sub(self.gwl[:], self.gwf[:], self.gwh[:]), "tV")
        self.v_gw_done = self.cnt["tV"]
        self.wload_done = {}

    def _load_weights(self, ph):
        g = self.nc.gpsimd
        m, q = divmod(ph, self.NQ)
        ps_, pe_ = self.qbounds[q]
        W = 2 * self.DT * 128
        w_sem = f"dW{ph % 2}"
        ins = g.dma_start(self.wguq[ph % 2][:, 0:(pe_ - ps_) * W],
                          self.wgu_h[m, :, ps_ * W: pe_ * W])
        self.inc(ins, w_sem, 16)
        ins = g.dma_start(self.wdq[ph % 2][:, 0:(pe_ - ps_) * self.D],
                          self.wd_h[m, :, ps_ * self.D: pe_ * self.D])
        self.wload_done[ph] = (w_sem, self.inc(ins, w_sem, 16))

    def p1(self):
        nc = self.nc
        g, v, s, t_e = nc.gpsimd, nc.vector, nc.scalar, nc.tensor
        D, H, TS = self.D, self.H, self.TS
        ha = self.h_aug.ap().rearrange("p (t h c) -> p t h c", h=H, c=65)
        self.v_p1_done = {}
        self.dA_h1 = {}
        invD = 1.0 / D
        for t in range(TS):
            if t >= 2:
                self.wait(g, "tV", self.v_p1_done[t - 2])
            gsem = f"dG{t % 2}"
            ins = g.dma_start(self.xt[t % 2][:],
                              self.x_h[128 * t:128 * (t + 1), :])
            g_load = self.inc(ins, gsem, 16)
            xb = self.xt[t % 2]
            xc = self.at[t % 2]
            self.wait(v, gsem, g_load)
            self.inc(v.reduce_sum(self.msum[:], xb[:], AX.X), "tV")
            self.wait(v, "tV", self.cnt["tV"])
            self.inc(v.tensor_scalar_mul(self.negmean[:], self.msum[:], -invD),
                     "tV")
            self.wait(v, "tV", self.cnt["tV"])
            vxc = self.inc(v.tensor_scalar_add(xc[:], xb[:], self.negmean[:]),
                           "tV")
            self.wait(s, "tV", vxc)
            ins = s.activation(self.sqps[:], xc[:], AF.Square,
                               accum_out=self.sumsq[:])
            self.inc(ins, "tS")
            self.wait(v, "tS", self.cnt["tS"])
            ins = v.tensor_scalar(self.varv[:], self.sumsq[:], invD, self.EPS,
                                  ALU.mult, ALU.add)
            self.inc(ins, "tV")
            self.wait(s, "tV", self.cnt["tV"])
            self.inc(s.activation(self.lnt[:], self.varv[:], AF.Ln), "tS")
            self.wait(s, "tS", self.cnt["tS"])
            srstd = self.inc(s.activation(self.rstd[:], self.lnt[:], AF.Exp,
                                          scale=-0.5), "tS")
            self.wait(v, "tS", srstd)
            for h in range(H):
                ins = v.tensor_scalar_mul(ha[:, t, h, 0:64],
                                          xc[:, 64 * h:64 * (h + 1)],
                                          self.rstd[:])
                self.inc(ins, "tV")
            v_haug = self.cnt["tV"]
            self.wait(s, "tV", v_haug)
            self.wait(s, "tG", self.g_ones)
            self.wait(s, "dZ", self.dZ_all)
            h1dst = self.h1_h[128 * t:128 * (t + 1), :].rearrange(
                "p (h c) -> p h c", h=H)[:, :, 0:65]
            ins = s.dma_start(h1dst, ha[:, t, :, 0:65])
            self.dA_h1[t] = self.inc(ins, "dA", 16)
            for h in range(H):
                ins = v.reduce_sum(self.n2h[:, h:h + 1],
                                   self.sqps[:, 64 * h:64 * (h + 1)], AX.X)
                self.inc(ins, "tV")
            self.inc(v.tensor_mul(self.rstd2[:], self.rstd[:], self.rstd[:]),
                     "tV")
            self.wait(v, "tV", self.cnt["tV"])
            vn2 = self.inc(v.tensor_scalar_mul(self.n2h[:], self.n2h[:],
                                               self.rstd2[:]), "tV")
            self.wait(s, "tV", vn2)
            self.inc(s.activation(self.n2h[:], self.n2h[:], AF.Ln), "tS")
            self.wait(s, "tS", self.cnt["tS"])
            snrm = self.inc(s.activation(self.nrmt[:], self.n2h[:], AF.Exp,
                                         scale=0.5), "tS")
            self.wait(v, "tS", snrm)
            self.inc(v.tensor_copy(self.norms[:, H * t:H * (t + 1)],
                                   self.nrmt[:]), "tV")
            self.v_p1_done[t] = self.cnt["tV"]
        self.wait(v, "tV", self.cnt["tV"])
        self.inc(v.reduce_max(self.mxp[:], self.norms[:, 0:TS * H], AX.X), "tV")
        self.wait(t_e, "tV", self.cnt["tV"])
        ins = t_e.transpose(self.tr2_ps[0:1, 0:128], self.mxp[:], self.identb[:])
        self.inc(ins, "tT")
        self.wait(v, "tT", self.cnt["tT"])
        self.inc(v.reduce_max(self.mx[:], self.tr2_ps[0:1, 0:128], AX.X), "tV")
        self.v_mx_done = self.cnt["tV"]
        self.dA_nrm = 0
        for t in range(TS):
            self.wait(t_e, "tV", self.v_mx_done)
            if t > 0:
                self.wait(t_e, "dA", self.dA_nrm)
            ins = t_e.transpose(self.tr2_ps[0:H, 0:128],
                                self.norms[:, H * t:H * (t + 1)], self.identb[:])
            self.inc(ins, "tT")
            self.wait(v, "tT", self.cnt["tT"])
            self.inc(v.tensor_copy(self.nT[:], self.tr2_ps[0:H, 0:128]), "tV")
            self.wait(s, "tV", self.cnt["tV"])
            ins = s.dma_start(self.nrm_h[t], self.nT[:])
            self.dA_nrm = self.inc(ins, "dA", 16)
        self.dA_nrm_all = self.cnt["dA"]

    def p2(self):
        nc = self.nc
        v, s, sy, t_e = nc.vector, nc.scalar, nc.sync, nc.tensor
        S, H, NB = self.S, self.H, self.NB
        ha = self.h_aug.ap().rearrange("p (t h c) -> p t h c", h=H, c=65)
        qk_last = {}
        dS_hT = {}
        v_bnd_done = {}
        pv_slab = {}
        exp_done = {}
        oevac = {}
        trp_done = {}
        aw_done = {}
        div_last = 0
        slab_rot = 0
        pair_ctr = 0

        def load_head(h):
            par = h % 2
            if h >= 2:
                self.wait(sy, "tT", qk_last[h - 2])
                self.wait(sy, "tV", v_bnd_done[h - 2])
            self.wait(sy, "dA", self.dA_nrm_all)
            hsem = f"dHT{par}"
            src = self.h1_h[:, 128 * h:128 * (h + 1)]
            ins = sy.dma_start(self.hT_on[par][:, :], src, transpose=True)
            self.inc(ins, hsem, 16)
            ins = sy.dma_start(self.hT_bd[par][:, :], src, transpose=True)
            self.inc(ins, hsem, 16)
            nsrc = self.nrm_h.rearrange("t h c -> h t c")[h:h + 1, :, :]
            bdst = self.bndr[par].ap().rearrange("o (t c) -> o t c", c=128)
            ins = sy.dma_start(bdst, nsrc)
            dS_hT[h] = (hsem, self.inc(ins, hsem, 16))

        load_head(0)
        if H > 1:
            load_head(1)
        for h in range(H):
            par = h % 2
            self.wait(v, dS_hT[h][0], dS_hT[h][1])
            self.wait(v, "tV", self.v_mx_done)
            if h >= 2:
                self.wait(v, "tT", qk_last[h - 2])
            for j in range(NB):
                ins = v.tensor_scalar(
                    self.hT_bd[par][64:65, 512 * j:512 * (j + 1)],
                    self.bndr[par][0:1, 512 * j:512 * (j + 1)],
                    self.mx[:], -1.0, ALU.mult, ALU.mult)
                self.inc(ins, "tV")
            v_bnd_done[h] = self.cnt["tV"]
            for j in range(NB):
                jg = h * NB + j
                for r in range(2 * j + 2):
                    qpar = pair_ctr % 2
                    pair_ctr += 1
                    self.wait(t_e, "tS", exp_done.get(qpar, 0))
                    self.wait(t_e, dS_hT[h][0], dS_hT[h][1])
                    self.wait(t_e, "tV", v_bnd_done[h])
                    if r == 0:
                        self.wait(t_e, "tV", oevac.get(jg % 2, 0))
                    for sfx in (0, 1):
                        kt = 2 * r + sfx
                        ins = t_e.matmul(
                            self.qk_ps[qpar][:, 512 * sfx:512 * (sfx + 1)],
                            self.hT_on[par][0:65, 128 * kt:128 * (kt + 1)],
                            self.hT_bd[par][0:65, 512 * j:512 * (j + 1)],
                            start=True, stop=True)
                    tqk = self.inc(ins, "tT")
                    self.wait(s, "tT", tqk)
                    self.wait(s, "tT", pv_slab.get(slab_rot - 3, 0))
                    ins = s.activation(self.pT[slab_rot % 3][:],
                                       self.qk_ps[qpar][:], AF.Exp, scale=0.125)
                    exp_done[qpar] = self.inc(ins, "tS")
                    if r >= 2 * j:
                        self.wait(v, "tS", exp_done[qpar])
                        for sfx in (0, 1):
                            var = 2 * r + sfx - 4 * j
                            ins = v.tensor_mul(
                                self.pT[slab_rot % 3][:, 512 * sfx:512 * (sfx + 1)],
                                self.pT[slab_rot % 3][:, 512 * sfx:512 * (sfx + 1)],
                                self.maskb[:, 512 * var:512 * (var + 1)])
                        vmask = self.inc(ins, "tV")
                        self.wait(t_e, "tV", vmask)
                    else:
                        self.wait(t_e, "tS", exp_done[qpar])
                    for sfx in (0, 1):
                        kt = 2 * r + sfx
                        ins = t_e.matmul(
                            self.o_ps[jg % 2][0:65, :],
                            ha[:, kt, h, :],
                            self.pT[slab_rot % 3][:, 512 * sfx:512 * (sfx + 1)],
                            start=(kt == 0), stop=(kt == 4 * j + 3))
                    pv_slab[slab_rot] = self.inc(ins, "tT")
                    slab_rot += 1
                self.wait(v, "tT", pv_slab[slab_rot - 1])
                self.wait(v, "tT", trp_done.get(jg - 2, 0))
                ins = v.tensor_copy(self.oT[jg % 2][0:65, :],
                                    self.o_ps[jg % 2][0:65, :])
                oevac[jg % 2] = self.inc(ins, "tV")
                self.wait(t_e, "tV", oevac[jg % 2])
                self.wait(t_e, "tV", div_last)
                for tt in range(4):
                    ins = t_e.transpose(
                        self.tr_ps[:, 128 * tt:128 * tt + 65],
                        self.oT[jg % 2][0:65, 128 * tt:128 * (tt + 1)],
                        self.identb[0:65, 0:65])
                trp_done[jg] = self.inc(ins, "tT")
                self.wait(v, "tT", trp_done[jg])
                if jg >= 2:
                    self.wait(v, f"dAO{jg % 2}", aw_done[jg - 2])
                trv = self.tr_ps.ap().rearrange("p (t c) -> p t c", c=128)
                self.inc(v.reciprocal(self.rcp4[:], trv[:, :, 64:65]), "tV")
                self.wait(v, "tV", self.cnt["tV"])
                for tt in range(4):
                    ins = v.tensor_scalar_mul(
                        self.asb[jg % 2][:, 64 * tt:64 * (tt + 1)],
                        self.tr_ps[:, 128 * tt:128 * tt + 64],
                        self.rcp4[:, tt:tt + 1])
                    self.inc(ins, "tV")
                div_last = self.cnt["tV"]
                self.wait(sy, "tV", div_last)
                adst = self.ao_h[512 * j:512 * (j + 1),
                                 64 * h:64 * (h + 1)].rearrange(
                    "(t p) c -> p t c", p=128)
                asrc = self.asb[jg % 2].ap().rearrange("p (t c) -> p t c", c=64)
                aw_done[jg] = self.inc(sy.dma_start(adst, asrc),
                                       f"dAO{jg % 2}", 16)
            qk_last[h] = self.cnt["tT"]
            if h + 2 < H:
                load_head(h + 2)
        self.t_attn_done = self.cnt["tT"]
        self.v_attn_done = self.cnt["tV"]
        self.s_attn_done = self.cnt["tS"]
        self.dAO_done = (self.cnt["dAO0"], self.cnt["dAO1"])
        if "aod" in self.dbg:
            self.wait(sy, "dAO0", self.dAO_done[0])
            self.wait(sy, "dAO1", self.dAO_done[1])
            self.inc(sy.dma_start(self.dbg["aod"], self.ao_h), "dS", 16)

    def p3(self):
        nc = self.nc
        g, v, s, sy = nc.gpsimd, nc.vector, nc.scalar, nc.sync
        D, TS = self.D, self.TS
        invD = 1.0 / D
        self.v_p3_done = {}
        self.dS_yinit = {}
        self.dA_h2 = {}
        self.wait(g, "dAO0", self.dAO_done[0])
        self.wait(g, "dAO1", self.dAO_done[1])
        for t in range(TS):
            gsem = f"dG{t % 2}"
            ysem = f"dY{t % 2}"
            if t >= 2:
                self.wait(g, "tV", self.v_p3_done[t - 2])
                self.wait(g, ysem, self.dS_yinit[t - 2])
            ins = g.dma_start(self.xt[t % 2][:], self.x_h[128 * t:128 * (t + 1), :])
            self.inc(ins, gsem, 16)
            ins = g.dma_start(self.at[t % 2][:], self.ao_h[128 * t:128 * (t + 1), :])
            gld = self.inc(ins, gsem, 16)
            xb, ab = self.xt[t % 2], self.at[t % 2]
            self.wait(v, gsem, gld)
            vx2 = self.inc(v.tensor_add(xb[:], xb[:], ab[:]), "tV")
            self.wait(sy, "tV", vx2)
            ins = sy.dma_start(self.y_h[128 * t:128 * (t + 1), :], xb[:])
            self.dS_yinit[t] = self.inc(ins, ysem, 16)
            self.wait(v, "tV", vx2)
            self.inc(v.reduce_sum(self.msum[:], xb[:], AX.X), "tV")
            self.wait(v, "tV", self.cnt["tV"])
            self.inc(v.tensor_scalar_mul(self.negmean[:], self.msum[:], -invD),
                     "tV")
            self.wait(v, "tV", self.cnt["tV"])
            vxc = self.inc(v.tensor_scalar_add(ab[:], xb[:], self.negmean[:]),
                           "tV")
            self.wait(s, "tV", vxc)
            ins = s.activation(self.sqps[:], ab[:], AF.Square,
                               accum_out=self.sumsq[:])
            self.inc(ins, "tS")
            self.wait(v, "tS", self.cnt["tS"])
            ins = v.tensor_scalar(self.varv[:], self.sumsq[:], invD, self.EPS,
                                  ALU.mult, ALU.add)
            self.inc(ins, "tV")
            self.wait(s, "tV", self.cnt["tV"])
            self.inc(s.activation(self.lnt[:], self.varv[:], AF.Ln), "tS")
            self.wait(s, "tS", self.cnt["tS"])
            srstd = self.inc(s.activation(self.rstd[:], self.lnt[:], AF.Exp,
                                          scale=-0.5), "tS")
            self.wait(v, "tS", srstd)
            if t >= 2:
                self.wait(v, f"dH2{t % 2}", self.dA_h2[t - 2])
            self.inc(v.tensor_scalar_mul(self.h2t[t % 2][:], ab[:], self.rstd[:]),
                     "tV")
            self.inc(v.tensor_scalar_mul(self.sqps[:], ab[:], self.rstd[:]),
                     "tV")
            self.wait(v, "tV", self.cnt["tV"])
            ins = v.tensor_sub(self.lot[t % 2][:], self.sqps[:], self.h2t[t % 2][:])
            self.v_p3_done[t] = self.inc(ins, "tV")
            self.wait(s, "tV", self.v_p3_done[t])
            h2sem = f"dH2{t % 2}"
            ins = s.dma_start(self.h2_h[128 * t:128 * (t + 1), :], self.h2t[t % 2][:])
            self.inc(ins, h2sem, 16)
            ins = s.dma_start(self.h2lo_h[128 * t:128 * (t + 1), :],
                              self.lot[t % 2][:])
            self.dA_h2[t] = self.inc(ins, h2sem, 16)
        self.dH2_all = (self.cnt["dH20"], self.cnt["dH21"])
        self.dY_all = (self.cnt["dY0"], self.cnt["dY1"])
        if "h2d" in self.dbg:
            self.wait(sy, "dH20", self.dH2_all[0])
            self.wait(sy, "dH21", self.dH2_all[1])
            self.inc(sy.dma_start(self.dbg["h2d"], self.h2_h), "dS", 16)

    def p4(self):
        nc = self.nc
        g, v, s, sy, t_e = nc.gpsimd, nc.vector, nc.scalar, nc.sync, nc.tensor
        S, D, TS, DT, NB = self.S, self.D, self.TS, self.DT, self.NB
        E, NM, NQ = self.E, self.NM, self.NQ

        self.esA.close()
        self.esAP.close()
        esB = ExitStack()
        xtT = self.sb("xtT", [128, DT * S], BF16, esB)
        self.wguq = [self.sb(f"wguq{i}", [128, self.QMAX * 2 * DT * 128],
                             BF16, esB) for i in (0, 1)]
        self.wdq = [self.sb(f"wdq{i}", [128, self.QMAX * D], BF16, esB)
                    for i in (0, 1)]
        self.wait(g, "tT", self.t_attn_done)
        self.wait(g, "tV", self.v_attn_done)
        self.wait(g, "tS", self.s_attn_done)
        for ph in range(min(2, self.NPH)):
            self._load_weights(ph)

        self.wait(sy, "tT", self.t_attn_done)
        self.wait(sy, "tV", self.v_attn_done)
        self.wait(sy, "tS", self.s_attn_done)
        self.wait(sy, "dH20", self.dH2_all[0])
        self.wait(sy, "dH21", self.dH2_all[1])
        for dt in range(DT):
            ins = sy.dma_start(xtT[:, S * dt:S * (dt + 1)],
                               self.h2_h[:, 128 * dt:128 * (dt + 1)],
                               transpose=True)
            self.inc(ins, "dXT", 16)
        dS_xtT = self.cnt["dXT"]

        esG = ExitStack()
        lo_ch = [self.sb(f"loch{i}", [128, DT * 512], BF16, esG) for i in (0, 1)]
        lgsb = self.sb("lgsb", [E, 512], F32, esG)
        lgh = self.sb("lgh", [E, 512], BF16, esG)
        lgl = self.sb("lgl", [E, 512], BF16, esG)
        lgt32 = self.sb("lgt32", [128, E], F32, esG)
        probs = self.sb("probs", [128, E], F32, esG)
        pr2 = self.sb("pr2", [128, E], F32, esG)
        ge1 = self.sb("ge1", [128, E], F32, esG)
        m12 = self.sb("m12", [128, E], F32, esG)
        negm1 = self.sb("negm1", [128, 1], F32, esG)
        zsum = self.sb("zsum", [128, 1], F32, esG)
        rz = self.sb("rz", [128, 1], F32, esG)
        mp1 = self.sb("mp1", [128, 1], F32, esG)
        mp2 = self.sb("mp2", [128, 1], F32, esG)
        dn = self.sb("dn", [128, 1], F32, esG)
        rd = self.sb("rd", [128, 1], F32, esG)
        esGP = ExitStack()
        lg_ps = self.ps("lg", [E, 512], F32, esGP)
        gt1_ps = self.ps("gt1", [128, E], BF16, esGP)
        gt2_ps = self.ps("gt2", [128, E], BF16, esGP)

        NCH = S // 512
        t_loch = {}
        v_gate_done = 0
        v_gt_used = 0
        v_lg_evac = 0
        for c in range(NCH):
            losem = f"dLo{c % 2}"
            if c >= 2:
                self.wait(sy, "tT", t_loch[c - 2])
            for dt in range(DT):
                ins = sy.dma_start(
                    lo_ch[c % 2][:, 512 * dt:512 * (dt + 1)],
                    self.h2lo_h[512 * c:512 * (c + 1), 128 * dt:128 * (dt + 1)],
                    transpose=True)
                self.inc(ins, losem, 16)
            dS_lo = self.cnt[losem]
            self.wait(t_e, "dXT", dS_xtT)
            self.wait(t_e, losem, dS_lo)
            self.wait(t_e, "tV", self.v_gw_done)
            if c > 0:
                self.wait(t_e, "tV", v_lg_evac)
            passes = [(self.gwh, "hi"), (self.gwl, "hi"), (self.gwh, "lo")]
            for pi, (wop, xop) in enumerate(passes):
                for dt in range(DT):
                    rhs = (xtT[:, S * dt + 512 * c: S * dt + 512 * (c + 1)]
                           if xop == "hi"
                           else lo_ch[c % 2][:, 512 * dt:512 * (dt + 1)])
                    ins = t_e.matmul(lg_ps[:], wop[:, E * dt:E * (dt + 1)], rhs,
                                     start=(pi == 0 and dt == 0),
                                     stop=(pi == 2 and dt == DT - 1))
            t_loch[c] = self.inc(ins, "tT")
            self.wait(v, "tT", t_loch[c])
            self.inc(v.tensor_copy(lgsb[:], lg_ps[:]), "tV")
            self.wait(v, "tV", self.cnt["tV"])
            self.inc(v.tensor_copy(lgh[:], lgsb[:]), "tV")
            self.wait(v, "tV", self.cnt["tV"])
            self.inc(v.tensor_sub(lgl[:], lgsb[:], lgh[:]), "tV")
            v_lg_evac = self.cnt["tV"]
            for tt in range(4):
                tile = 4 * c + tt
                self.wait(t_e, "tV", v_lg_evac)
                self.wait(t_e, "tV", v_gt_used)
                ins = t_e.transpose(gt1_ps[:], lgh[0:E, 128 * tt:128 * (tt + 1)],
                                    self.identb[0:E, 0:E])
                self.inc(ins, "tT")
                ins = t_e.transpose(gt2_ps[:], lgl[0:E, 128 * tt:128 * (tt + 1)],
                                    self.identb[0:E, 0:E])
                tgt = self.inc(ins, "tT")
                self.wait(v, "tT", tgt)
                self.inc(v.tensor_copy(lgt32[:], gt1_ps[:]), "tV")
                self.wait(v, "tV", self.cnt["tV"])
                self.inc(v.tensor_add(lgt32[:], lgt32[:], gt2_ps[:]), "tV")
                v_gt_used = self.cnt["tV"]
                self.wait(v, "tV", v_gt_used)
                self.inc(v.reduce_max(negm1[:], lgt32[:], AX.X, negate=True),
                         "tV")
                self.wait(s, "tV", self.cnt["tV"])
                ins = s.activation(probs[:], lgt32[:], AF.Exp, bias=negm1[:],
                                   accum_out=zsum[:])
                self.inc(ins, "tS")
                self.wait(v, "tS", self.cnt["tS"])
                self.inc(v.reciprocal(rz[:], zsum[:]), "tV")
                self.wait(v, "tV", self.cnt["tV"])
                self.inc(v.tensor_scalar_mul(probs[:], probs[:], rz[:]), "tV")
                self.wait(v, "tV", self.cnt["tV"])
                self.inc(v.reduce_max(mp1[:], probs[:], AX.X), "tV")
                self.wait(v, "tV", self.cnt["tV"])
                self.inc(v.tensor_single_scalar(ge1[:], probs[:], mp1[:],
                                                ALU.is_ge), "tV")
                self.wait(v, "tV", self.cnt["tV"])
                ins = v.scalar_tensor_tensor(pr2[:], ge1[:], -1e30, probs[:],
                                             ALU.mult, ALU.add)
                self.inc(ins, "tV")
                self.wait(v, "tV", self.cnt["tV"])
                self.inc(v.reduce_max(mp2[:], pr2[:], AX.X), "tV")
                self.wait(v, "tV", self.cnt["tV"])
                self.inc(v.tensor_single_scalar(m12[:], probs[:], mp2[:],
                                                ALU.is_ge), "tV")
                self.inc(v.tensor_add(dn[:], mp1[:], mp2[:]), "tV")
                self.wait(v, "tV", self.cnt["tV"])
                self.inc(v.reciprocal(rd[:], dn[:]), "tV")
                self.wait(v, "tV", self.cnt["tV"])
                ins = v.scalar_tensor_tensor(self.cw[:, E * tile:E * (tile + 1)],
                                             probs[:], rd[:], m12[:],
                                             ALU.mult, ALU.mult)
                v_gate_done = self.inc(ins, "tV")
        t_gate_done = self.cnt["tT"]
        if "cwd" in self.dbg:
            cwsrc = self.cw.ap().rearrange("p (t e) -> p t e", e=E)
            cwdst = self.dbg["cwd"].rearrange("(t p) e -> p t e", p=128)
            self.wait(sy, "tV", v_gate_done)
            self.inc(sy.dma_start(cwdst, cwsrc), "dS", 16)
        esG.close()
        esGP.close()

        sg = [self.sb(f"sg{i}", [128, 512], BF16, esB) for i in (0, 1)]
        a_full = [self.sb(f"af{i}", [128, self.QMAX * 512], BF16, esB)
                  for i in (0, 1)]
        NACC = 3
        accb = [self.sb(f"acc{i}", [128, 2048], F32, esB) for i in range(NACC)]
        esBP = ExitStack()
        gu_ps = [self.ps(f"gu{i}", [128, 512], F32, esBP) for i in range(4)]
        y_ps = [self.ps(f"y{i}", [128, 1024], F32, esBP) for i in (0, 1)]

        t_dn = {}
        v_mult = {}
        gu_user = {}
        acc_out_hist = {}
        acc_out_rows = {}
        dA_acc = {}
        self.t_wphase_done = {}
        y_user = 0

        def quarter(ph):
            m, q = divmod(ph, NQ)
            ps_, pe_ = self.qbounds[q]
            return m, q, ps_, pe_

        def emit_accin(gbb, bb, half, slot):
            self.wait(s, "dY0", self.dY_all[0])
            self.wait(s, "dY1", self.dY_all[1])
            if slot - NACC in acc_out_hist:
                sm, vv = acc_out_hist[slot - NACC]
                self.wait(s, sm, vv)
            if (bb, half) in acc_out_rows:
                sm, vv = acc_out_rows[(bb, half)]
                self.wait(s, sm, vv)
            self.wait(s, "tT", t_gate_done)
            self.wait(s, "tV", v_gate_done)
            row = 512 * bb + 256 * half
            src = self.y_h[row:row + 256, :].rearrange("(t p) c -> p t c", p=128)
            dst = accb[slot % NACC].ap().rearrange("p (t c) -> p t c", c=D)
            dA_acc[slot] = self.inc(s.dma_start(dst, src), f"dAC{slot % NACC}",
                                    16)

        def emit_up(ph, gbb, bb, iloc, gpar):
            self.wait(t_e, *self.wload_done[ph])
            self.wait(t_e, "dXT", dS_xtT)
            self.wait(t_e, "tV", gu_user.get(gpar, 0))
            if gbb == 0 and iloc == 0:
                self.wait(t_e, "tV", v_gate_done)
            for gu in (0, 1):
                for dt in range(DT):
                    col = ((iloc * 2 + gu) * DT + dt) * 128
                    ins = t_e.matmul(
                        gu_ps[2 * gpar + gu][:],
                        self.wguq[ph % 2][:, col:col + 128],
                        xtT[:, S * dt + 512 * bb: S * dt + 512 * (bb + 1)],
                        start=(dt == 0), stop=(dt == DT - 1))
            t_up = self.inc(ins, "tT")
            self.wait(s, "tT", t_up)
            self.wait(s, "tV", gu_user.get(gpar, 0))
            ins = s.activation(sg[gpar][:], gu_ps[2 * gpar][:], AF.Silu)
            s_silu = self.inc(ins, "tS")
            self.wait(v, "tS", s_silu)
            self.wait(v, "tT", t_dn.get((gbb - 2, 1), 0))
            ins = v.tensor_mul(a_full[gbb % 2][:, 512 * iloc:512 * (iloc + 1)],
                               sg[gpar][:], gu_ps[2 * gpar + 1][:])
            v_mult[(gbb, iloc)] = self.inc(ins, "tV")
            gu_user[gpar] = v_mult[(gbb, iloc)]

        def emit_down(ph, gbb, bb, half, ni):
            self.wait(t_e, "tV", v_mult[(gbb, ni - 1)])
            self.wait(t_e, "tV", y_user)
            for iloc in range(ni):
                for ts2 in (0, 1):
                    tt = half * 2 + ts2
                    lhs = a_full[gbb % 2][:, 512 * iloc + 128 * tt:
                                          512 * iloc + 128 * (tt + 1)]
                    for ch in (0, 1):
                        ins = t_e.matmul(
                            y_ps[ts2][:, 512 * ch:512 * (ch + 1)], lhs,
                            self.wdq[ph % 2][:, D * iloc + 512 * ch:
                                             D * iloc + 512 * (ch + 1)],
                            start=(iloc == 0), stop=(iloc == ni - 1))
            t_dn[(gbb, half)] = self.inc(ins, "tT")

        def emit_evac(ph, gbb, bb, half, slot):
            nonlocal y_user
            m = quarter(ph)[0]
            self.wait(v, "tT", t_dn[(gbb, half)])
            self.wait(v, f"dAC{slot % NACC}", dA_acc[slot])
            ab = accb[slot % NACC]
            for ts2 in (0, 1):
                tile = bb * 4 + half * 2 + ts2
                for ch in (0, 1):
                    dsl = ab[:, D * ts2 + 512 * ch: D * ts2 + 512 * (ch + 1)]
                    ysl = y_ps[ts2][:, 512 * ch:512 * (ch + 1)]
                    if m == 0:
                        ins = v.tensor_add(dsl, dsl, ysl)
                    else:
                        col = E * tile + (m - 1)
                        ins = v.scalar_tensor_tensor(
                            dsl, ysl, self.cw[:, col:col + 1], dsl,
                            ALU.mult, ALU.add)
            v_evac = self.inc(ins, "tV")
            y_user = v_evac
            self.wait(sy, "tV", v_evac)
            row = 512 * bb + 256 * half
            dst = self.y_h[row:row + 256, :].rearrange("(t p) c -> p t c", p=128)
            src = ab.ap().rearrange("p (t c) -> p t c", c=D)
            ins = sy.dma_start(dst, src)
            stsem = f"dST{slot % NACC}"
            acc_out_hist[slot] = (stsem, self.inc(ins, stsem, 16))
            acc_out_rows[(bb, half)] = acc_out_hist[slot]

        blocks = []
        for ph in range(self.NPH):
            ni = quarter(ph)[3] - quarter(ph)[2]
            for bb in range(NB):
                blocks.append((ph, bb, ni))
        pend = None
        gpar_ctr = 0
        for gbb, (ph, bb, ni) in enumerate(blocks):
            emit_accin(gbb, bb, 0, 2 * gbb)
            split = (ni + 1) // 2
            for iloc in range(split):
                emit_up(ph, gbb, bb, iloc, gpar_ctr % 2)
                gpar_ctr += 1
            if pend is not None:
                pph, pgbb, pbb, pni = pend
                emit_down(pph, pgbb, pbb, 0, pni)
                emit_evac(pph, pgbb, pbb, 0, 2 * pgbb)
            emit_accin(gbb, bb, 1, 2 * gbb + 1)
            for iloc in range(split, ni):
                emit_up(ph, gbb, bb, iloc, gpar_ctr % 2)
                gpar_ctr += 1
            if pend is not None:
                pph, pgbb, pbb, pni = pend
                emit_down(pph, pgbb, pbb, 1, pni)
                emit_evac(pph, pgbb, pbb, 1, 2 * pgbb + 1)
                if pbb == NB - 1:
                    self.t_wphase_done[pph] = self.cnt["tT"]
                    if pph + 2 < self.NPH:
                        self.wait(g, "tT", self.t_wphase_done[pph])
                        self._load_weights(pph + 2)
            pend = (ph, gbb, bb, ni)
        pph, pgbb, pbb, pni = pend
        for half in (0, 1):
            emit_down(pph, pgbb, pbb, half, pni)
            emit_evac(pph, pgbb, pbb, half, 2 * pgbb + half)
        esBP.close()
        esB.close()


def build(cfg, debug_outputs=()):
    return MK(cfg, debug_outputs).build()



def kernel(L0_attn_x, L0_ffn_gate_w, L0_ffn_base_wgu, L0_ffn_base_wd,
           L0_ffn_exp_wgu, L0_ffn_exp_wd):
    from concourse.bass_utils import run_bass_kernel_spmd

    X = np.asarray(L0_attn_x, dtype=np.float32)
    gw = np.ascontiguousarray(np.asarray(L0_ffn_gate_w, dtype=np.float32))
    base_wgu = np.asarray(L0_ffn_base_wgu, dtype=np.float32)
    base_wd = np.asarray(L0_ffn_base_wd, dtype=np.float32)
    exp_wgu = np.asarray(L0_ffn_exp_wgu, dtype=np.float32)
    exp_wd = np.asarray(L0_ffn_exp_wd, dtype=np.float32)

    cfg = default_cfg()
    ws_gu = [base_wgu] + [exp_wgu[e] for e in range(exp_wgu.shape[0])]
    ws_d = [base_wd] + [exp_wd[e] for e in range(exp_wd.shape[0])]
    wgu_all, wd_all = pack_weights(ws_gu, ws_d, cfg)

    nc = build(cfg)
    n_cores = X.shape[0]
    in_maps = [{"x": np.ascontiguousarray(X[c]), "gw": gw,
                "wgu": wgu_all, "wd": wd_all} for c in range(n_cores)]
    res = run_bass_kernel_spmd(nc, in_maps, core_ids=list(range(n_cores)))
    out = np.stack([np.asarray(r["y"], dtype=np.float32)
                    for r in res.results], axis=0)
    return out
